# version 02 (final)
# baseline (speedup 1.0000x reference)
import numpy as np
import ml_dtypes
import concourse.bass as bass
import concourse.mybir as mybir
from concourse.bass_utils import run_bass_kernel_spmd

F32 = mybir.dt.float32
BF16 = mybir.dt.bfloat16
ACT = mybir.ActivationFunctionType
ALU = mybir.AluOpType
AX = mybir.AxisListType.X
ts = bass.ts

S = 2048
DM = 4096
NT = 16
NDC = 32
NH = 16
NKV = 4
FPC = 2048
ISCALE = float(1.0 / np.sqrt(128.0))

ENGINES = ("sync", "gpsimd", "tensor", "scalar", "vector")


class Prog:

    def __init__(self):
        self.q = {e: [] for e in ENGINES}
        self.cnt = {}
        self.gi = 0
        self.lane_i = 0

    def val(self, sem):
        return self.cnt.get(sem, 0)

    def op(self, eng, fn, waits=(), inc=None, amt=1, dr=False):
        w = tuple((s, v) for (s, v) in waits if v and v > 0)
        self.q[eng].append((fn, w, inc, amt, dr))
        if inc is not None:
            self.cnt[inc] = self.cnt.get(inc, 0) + amt
            return self.cnt[inc]
        return None


def build_program(debug=False):
    nc = bass.Bass("TRN2", target_bir_lowering=False)

    x_d = nc.dram_tensor("x", [S, DM], F32, kind="ExternalInput").ap()
    xr_d = nc.dram_tensor("xr", [S, FPC], F32, kind="ExternalInput").ap()
    wq_d = nc.dram_tensor("wq", [NH, 128, 4096], F32, kind="ExternalInput").ap()
    wk_d = nc.dram_tensor("wk", [NKV, 128, 4096], F32, kind="ExternalInput").ap()
    wv_d = nc.dram_tensor("wv", [NKV, 128, 4096], F32, kind="ExternalInput").ap()
    wb_d = nc.dram_tensor("wb", [8, 128, 8192], F32, kind="ExternalInput").ap()
    wr_d = nc.dram_tensor("wr", [128, 256], F32, kind="ExternalInput").ap()
    la_d = nc.dram_tensor("la", [128, 4096], F32, kind="ExternalInput").ap()
    lb_d = nc.dram_tensor("lb", [128, FPC], F32, kind="ExternalInput").ap()
    bq_d = nc.dram_tensor("bq", [128, 16], F32, kind="ExternalInput").ap()
    bk_d = nc.dram_tensor("bk", [128, 4], F32, kind="ExternalInput").ap()
    bv_d = nc.dram_tensor("bv", [128, 4], F32, kind="ExternalInput").ap()
    br_d = nc.dram_tensor("br", [8, 1], F32, kind="ExternalInput").ap()
    bb_d = nc.dram_tensor("bb", [1, FPC], F32, kind="ExternalInput").ap()
    cos_d = nc.dram_tensor("cos2", [128, S], BF16, kind="ExternalInput").ap()
    sin_d = nc.dram_tensor("sin2", [128, S], BF16, kind="ExternalInput").ap()
    idb_d = nc.dram_tensor("idb", [128, 128], BF16, kind="ExternalInput").ap()
    idf_d = nc.dram_tensor("idf", [128, 128], F32, kind="ExternalInput").ap()
    oneb_d = nc.dram_tensor("oneb", [1, 128], BF16, kind="ExternalInput").ap()
    em_d = nc.dram_tensor("emat", [8, 128], BF16, kind="ExternalInput").ap()

    qsp_d = nc.dram_tensor("qspill", [NH, 128, S], BF16, kind="Internal").ap()
    vsp_d = nc.dram_tensor("vspill", [NKV, 128, 16 * 132], BF16, kind="Internal").ap()
    gsp_d = nc.dram_tensor("gspill", [NT, 128, FPC], F32, kind="Internal").ap()
    out_d = nc.dram_tensor("out", [S, FPC], F32, kind="ExternalOutput").ap()

    dbg = {}
    if debug:
        dbg["xn"] = nc.dram_tensor("d_xn", [128, NDC * S], BF16, kind="ExternalOutput").ap()
        dbg["kp"] = nc.dram_tensor("d_kp", [128, NKV * S], BF16, kind="ExternalOutput").ap()
        dbg["w"] = nc.dram_tensor("d_w", [128, NT * 8], F32, kind="ExternalOutput").ap()
        dbg["wh"] = nc.dram_tensor("d_wh", [128, S], BF16, kind="ExternalOutput").ap()
        dbg["s"] = nc.dram_tensor("d_s", [512, FPC], F32, kind="ExternalOutput").ap()

    from contextlib import ExitStack
    stack = ExitStack()
    en = stack.enter_context

    ar16 = en(nc.sbuf_tensor("ar16", [128, 73728], BF16))
    xr32 = en(nc.sbuf_tensor("xr32", [128, 8192], F32))
    gstage = en(nc.sbuf_tensor("gstage", [128, 512], F32))
    wh_sb = en(nc.sbuf_tensor("wh_sb", [128, FPC], BF16))
    wrep_sb = en(nc.sbuf_tensor("wrep_sb", [128, FPC], BF16))
    wf_sb = en(nc.sbuf_tensor("wf_sb", [8, FPC], BF16))
    cos_sb = en(nc.sbuf_tensor("cos_sb", [128, S], BF16))
    sin_sb = en(nc.sbuf_tensor("sin_sb", [128, S], BF16))
    wr_sb = en(nc.sbuf_tensor("wr_sb", [128, 256], BF16))
    bq_sb = en(nc.sbuf_tensor("bq_sb", [128, 16], F32))
    bk_sb = en(nc.sbuf_tensor("bk_sb", [128, 4], F32))
    bv_sb = en(nc.sbuf_tensor("bv_sb", [128, 4], F32))
    br_sb = en(nc.sbuf_tensor("br_sb", [8, 1], F32))
    bb_sb = en(nc.sbuf_tensor("bb_sb", [1, FPC], BF16))
    idb_sb = en(nc.sbuf_tensor("idb_sb", [128, 128], BF16))
    idf_sb = en(nc.sbuf_tensor("idf_sb", [128, 128], F32))
    oneb_sb = en(nc.sbuf_tensor("oneb_sb", [1, 128], BF16))
    em_sb = en(nc.sbuf_tensor("em_sb", [8, 128], BF16))
    st_sb = en(nc.sbuf_tensor("st_sb", [128, 2, 8], F32))
    sm_sb = en(nc.sbuf_tensor("sm_sb", [128, 40], F32))
    rc_sb = en(nc.sbuf_tensor("rc_sb", [128, 8], F32))

    A16 = ar16.ap()
    xnF = A16[:, 0:65536].rearrange("p (c t) -> p c t", c=NDC)
    xn_tm = A16[:, 65536:73728].rearrange("p (b t) -> p b t", b=2)
    la_sb = A16[:, 65536:69632].rearrange("p (c r) -> p c r", c=NDC)
    lb_sb = A16[:, 69632:71680].rearrange("p (g f) -> p g f", g=8)
    kp_sb = A16[:, 65536:73728].rearrange("p (v t) -> p v t", v=NKV)
    v3_sb = A16[:, 0:8448].rearrange("p (v t) -> p v t", v=NKV)
    pr_sb = A16[:, 8448:9472].rearrange("p (b t) -> p b t", b=2)
    qh_sb = A16[:, 9472:10496].rearrange("p (b t) -> p b t", b=2)
    A16f = A16.bitcast(F32)
    s_sb = A16f[:, 8192:16384].rearrange("p (j t) -> p j t", j=4)
    g_sb = A16f[:, 16384:24576].rearrange("p (b t) -> p b t", b=4)
    xe_sb = A16f[:, 24576:32768].rearrange("p (b t) -> p b t", b=4)

    XR = xr32.ap()
    x_sb = XR.rearrange("p (b t) -> p b t", b=2)
    lg_sb = XR[:, 0:2048]
    zT_sb = XR[:, 2048:2176].rearrange("p (c e) -> p c e", c=16)
    wT_sb = XR[:, 2176:2304].rearrange("p (c e) -> p c e", c=16)
    R_sb = XR[:, 2304:2368]
    XRb = XR.bitcast(BF16)
    wffn = XRb[:, 0:16384].rearrange("p (b c f) -> p b c f", b=2, c=NDC)
    wsl = XRb[:, 0:8192].rearrange("p (b c f) -> p b c f", b=2, c=NDC)
    stg = XRb[:, 8192:12288].rearrange("p (b t) -> p b t", b=2)
    vstage = XRb[:, 12288:14400].rearrange("p (c d) -> p c d", c=16)
    ta_sb = XRb[:, 12288:14336]
    tb_sb = XRb[:, 14336:16384]

    pp = [en(nc.psum_tensor(f"pp{i}", [128, 512], F32)) for i in range(4)]
    ptb = [en(nc.psum_tensor(f"ptb{i}", [128, 1024], BF16)) for i in range(2)]
    pt2 = en(nc.psum_tensor("pt2", [128, 512], F32))
    pex = en(nc.psum_tensor("pex", [128, 512], F32))

    NLANES = 12
    sems = {}
    for name in ("T", "SC", "V"):
        sems[name] = en(nc.semaphore(name))
    for i in range(NLANES):
        sems[f"L{i}"] = en(nc.semaphore(f"L{i}"))

    P = Prog()

    def gdma(out, in_, waits=(), **kw):
        P.gi += 1
        name = f"g{P.gi}"
        sems[name] = en(nc.semaphore(name))
        P.op("gpsimd", lambda e, o=out, i=in_, kw=kw: e.dma_start(out=o, in_=i, **kw),
             waits=waits, inc=name, amt=16)
        return (name, 16)

    def sdma(out, in_, waits=()):
        ln = f"L{P.lane_i % NLANES}"
        P.lane_i += 1
        w = list(waits) + [(ln, P.val(ln))]
        v = P.op("sync", lambda e, o=out, i=in_: e.dma_start(out=o, in_=i),
                 waits=w, inc=ln, amt=16)
        return (ln, v)

    c_cos = sdma(cos_sb.ap(), cos_d)
    c_sin = sdma(sin_sb.ap(), sin_d)
    c_idb = sdma(idb_sb.ap(), idb_d)
    c_idf = sdma(idf_sb.ap(), idf_d)
    c_oneb = sdma(oneb_sb.ap(), oneb_d)
    c_em = sdma(em_sb.ap(), em_d)
    c_bq = sdma(bq_sb.ap(), bq_d)
    c_bk = sdma(bk_sb.ap(), bk_d)
    c_bv = sdma(bv_sb.ap(), bv_d)
    c_br = sdma(br_sb.ap(), br_d)
    w_wr = gdma(wr_sb.ap(), wr_d)
    w_bb = gdma(bb_sb.ap(), bb_d)

    sy_x = [0] * NT
    sc_xn = [0] * NT
    v_st = [0] * NT
    t_tr = [0] * (NT * 4)
    v_ev = [0] * (NT * 4)

    def load_x(i, w):
        sy_x[i] = sdma(x_sb[:, i % 2, :], x_d[ts(i, 128), :], waits=w)

    load_x(0, [])
    load_x(1, [])

    for i in range(NT):
        p = i % 2
        st = st_sb.ap()[:, p, :]
        rs, mean, msq, ssq, ve, rcs, r, negmr = (st[:, j:j + 1] for j in range(8))
        xv = x_sb[:, p, :]
        P.op("vector", lambda e, o=rs, i_=xv: e.reduce_sum(o, i_, axis=AX),
             waits=[sy_x[i]])
        P.op("vector", lambda e, o=mean, i_=rs: e.tensor_scalar_mul(o, i_, 1.0 / DM))
        P.op("vector", lambda e, o=msq, a=mean: e.tensor_mul(o, a, a))
        wsq = [sy_x[i]]
        if i >= 2:
            wsq.append(("T", t_tr[(i - 2) * 4 + 3]))
        sc_sq = P.op("scalar",
                     lambda e, o=xn_tm[:, p, :], i_=xv, a=ssq:
                     e.activation(o, i_, ACT.Square, accum_out=a),
                     waits=wsq, inc="SC", dr=True)
        P.op("vector", lambda e, o=ve, i_=ssq: e.tensor_scalar_mul(o, i_, 1.0 / DM),
             waits=[("SC", sc_sq)])
        P.op("vector", lambda e, o=ve, i_=ve, m=msq:
             e.scalar_tensor_tensor(o, i_, 1e-5, m, ALU.add, ALU.subtract))
        v_rc = P.op("vector", lambda e, o=rcs, i_=ve: e.reciprocal(o, i_), inc="V")
        sc_r = P.op("scalar", lambda e, o=r, i_=rcs: e.activation(o, i_, ACT.Sqrt),
                    waits=[("V", v_rc)], inc="SC")
        v_st[i] = P.op("vector", lambda e, o=negmr, m=mean, r_=r:
                       e.scalar_tensor_tensor(o, m, -1.0, r_, ALU.mult, ALU.mult),
                       waits=[("SC", sc_r)], inc="V")
        sc_xn[i] = P.op("scalar", lambda e, o=xn_tm[:, p, :], i_=xv, r_=r, b=negmr:
                        e.activation(o, i_, ACT.Identity, bias=b, scale=r_),
                        waits=[("V", v_st[i])], inc="SC", dr=True)
        for gq in range(4):
            bank = ptb[gq % 2]
            w = [("SC", sc_xn[i]), c_idb]
            gprev = i * 4 + gq - 2
            if gprev >= 0:
                w.append(("V", v_ev[gprev]))
            t = 0
            for j in range(8):
                dc = gq * 8 + j
                inc = "T" if j == 7 else None
                t = P.op("tensor",
                         lambda e, o=bank.ap()[:, ts(j, 128)],
                         i_=xn_tm[:, p, ts(dc, 128)]:
                         e.transpose(o, i_, idb_sb.ap()),
                         waits=(w if j == 0 else ()), inc=inc)
            t_tr[i * 4 + gq] = t
            v_ev[i * 4 + gq] = P.op(
                "vector",
                lambda e, o=xnF[:, gq * 8:(gq + 1) * 8, ts(i, 128)],
                b=bank.ap().rearrange("p (j t) -> p j t", j=8):
                e.tensor_copy(o, b),
                waits=[("T", t_tr[i * 4 + gq])], inc="V")
        if i + 2 < NT:
            load_x(i + 2, [("SC", sc_xn[i]), ("V", v_st[i])])

    t_p1_end = P.val("T")
    v_p1_end = P.val("V")
    sc_p1_end = P.val("SC")

    if debug:
        sdma(dbg["xn"], xnF.rearrange("p c t -> p (c t)"), waits=[("V", v_p1_end)])

    w_la = gdma(la_sb, la_d, waits=[("T", t_p1_end)])
    w_lb = gdma(lb_sb.rearrange("p g f -> p (g f)"), lb_d, waits=[("T", t_p1_end)])

    def xn_ready(tg):
        return ("V", v_ev[(4 * tg + 3) * 4 + 3])

    xn_all = ("V", v_p1_end)
    wrv = wr_sb.ap().rearrange("p (c e) -> p c e", c=NDC)

    t_lg = [0] * 4
    for tg in range(4):
        for dc in range(NDC):
            inc = "T" if dc == NDC - 1 else None
            w = [xn_ready(tg), w_wr] if dc == 0 else ()
            t_lg[tg] = P.op("tensor",
                            lambda e, o=pp[tg].ap()[0:8, :], l=wrv[:, dc, :],
                            r=xnF[:, dc, ts(tg, 512)], s=(dc == 0), t_=(dc == NDC - 1):
                            e.matmul(o, l, r, start=s, stop=t_),
                            waits=w, inc=inc)
    sc_lg = [0] * 4
    for tg in range(4):
        sc_lg[tg] = P.op("scalar",
                         lambda e, o=lg_sb[0:8, ts(tg, 512)], i_=pp[tg].ap()[0:8, :]:
                         e.activation(o, i_, ACT.Identity, bias=br_sb.ap()),
                         waits=[("T", t_lg[tg]), c_br, ("V", v_p1_end)],
                         inc="SC")
    t_zt = [0] * 2
    v_zt = [0] * 2
    for g in range(2):
        for j in range(8):
            tc = g * 8 + j
            w = [("SC", sc_lg[tc // 4]), c_idf]
            if g == 1:
                w.append(("V", v_zt[0]))
            inc = "T" if j == 7 else None
            t_zt[g] = P.op("tensor",
                           lambda e, o=pt2.ap()[:, ts(j, 8)], i_=lg_sb[0:8, ts(tc, 128)]:
                           e.transpose(o, i_, idf_sb.ap()[0:8, 0:8]),
                           waits=(w if j == 0 else [("SC", sc_lg[tc // 4])]), inc=inc)
        v_zt[g] = P.op("vector",
                       lambda e, o=zT_sb[:, g * 8:(g + 1) * 8, :],
                       i_=pt2.ap()[:, 0:64].rearrange("p (c e) -> p c e", c=8):
                       e.tensor_copy(o, i_),
                       waits=[("T", t_zt[g])], inc="V")
    sm = sm_sb.ap()
    fi, sig, kk, sz, kin, om, ntau = (sm[:, 0:8], sm[:, 8:16], sm[:, 16:17],
                                      sm[:, 17:18], sm[:, 18:19], sm[:, 19:20],
                                      sm[:, 20:21])
    Rv = R_sb.rearrange("p (a b) -> p a b", a=8)
    sc_w = [0] * NT
    for tc in range(NT):
        z = zT_sb[:, tc, :]
        w = [("V", v_zt[tc // 8])]
        for i_e in range(8):
            P.op("vector",
                 lambda e, o=Rv[:, i_e, :], z_=z, s1=zT_sb[:, tc, i_e:i_e + 1]:
                 e.tensor_scalar(o, z_, s1, 0.0, ALU.subtract, ALU.max),
                 waits=(w if i_e == 0 else ()))
        P.op("vector", lambda e, o=fi, i_=Rv: e.reduce_sum(o, i_, axis=AX))
        P.op("vector", lambda e, o=sig, i_=fi:
             e.tensor_single_scalar(o, i_, 1.0, ALU.is_lt))
        P.op("vector", lambda e, o=kk, i_=sig: e.reduce_sum(o, i_, axis=AX))
        P.op("vector", lambda e, o=sm[:, 24:32], a=sig, b=z: e.tensor_mul(o, a, b))
        P.op("vector", lambda e, o=sz, i_=sm[:, 24:32]: e.reduce_sum(o, i_, axis=AX))
        P.op("vector", lambda e, o=kin, i_=kk: e.reciprocal(o, i_))
        P.op("vector", lambda e, o=om, i_=sz:
             e.tensor_scalar(o, i_, -1.0, 1.0, ALU.mult, ALU.add))
        v_nt = P.op("vector", lambda e, o=ntau, a=om, b=kin: e.tensor_mul(o, a, b),
                    waits=([("SC", sc_w[tc - 1])] if tc > 0 else []), inc="V")
        sc_w[tc] = P.op("scalar", lambda e, o=wT_sb[:, tc, :], i_=z, b=ntau:
                        e.activation(o, i_, ACT.Relu, bias=b),
                        waits=[("V", v_nt)], inc="SC")
    t_wt = [0] * 4
    v_wf = [0] * 4
    for g in range(4):
        for j in range(4):
            tc = g * 4 + j
            inc = "T" if j == 3 else None
            w = [("SC", sc_w[tc]),
                 ("V", v_zt[1] if g == 0 else v_wf[g - 1])]
            t_wt[g] = P.op("tensor",
                           lambda e, o=pt2.ap()[0:8, ts(j, 128)], i_=wT_sb[:, tc, :]:
                           e.transpose(o, i_, idf_sb.ap()),
                           waits=(w if j == 0 else [("SC", sc_w[tc])]), inc=inc)
        v_wf[g] = P.op("vector",
                       lambda e, o=wf_sb.ap()[0:8, ts(g, 512)], i_=pt2.ap()[0:8, :]:
                       e.tensor_copy(o, i_),
                       waits=[("T", t_wt[g])], inc="V")
    t_wr2 = [0] * 4
    v_wrep = [0] * 4
    for tg in range(4):
        t_wr2[tg] = P.op("tensor",
                         lambda e, o=pp[tg].ap(), l=em_sb.ap(),
                         r=wf_sb.ap()[0:8, ts(tg, 512)]:
                         e.matmul(o, l, r, start=True, stop=True),
                         waits=[("V", v_wf[tg]), c_em, ("SC", sc_lg[tg])],
                         inc="T")
        v_wrep[tg] = P.op("vector",
                          lambda e, o=wrep_sb.ap()[:, ts(tg, 512)], i_=pp[tg].ap():
                          e.tensor_copy(o, i_),
                          waits=[("T", t_wr2[tg])], inc="V")
    dbg_w = None
    if debug:
        dbg_w = sdma(dbg["w"], wT_sb.rearrange("p c e -> p (c e)"),
                     waits=[("SC", sc_w[15])])

    t_h = [0] * 4
    v_wh = [0] * 4
    for tg in range(4):
        for dc in range(NDC):
            inc = "T" if dc == NDC - 1 else None
            w = [xn_all, w_la, ("V", v_wrep[tg])] if dc == 0 else ()
            t_h[tg] = P.op("tensor",
                           lambda e, o=pp[tg].ap(), l=la_sb[:, dc, :],
                           r=xnF[:, dc, ts(tg, 512)], s=(dc == 0), t_=(dc == NDC - 1):
                           e.matmul(o, l, r, start=s, stop=t_),
                           waits=w, inc=inc)
        v_wh[tg] = P.op("vector",
                        lambda e, o=wh_sb.ap()[:, ts(tg, 512)], a=pp[tg].ap(),
                        b=wrep_sb.ap()[:, ts(tg, 512)]:
                        e.tensor_tensor(o, a, b, ALU.mult),
                        waits=[("T", t_h[tg])], inc="V")
    if debug:
        sdma(dbg["wh"], wh_sb.ap(), waits=[("V", v_wh[3])])

    w_ffn = [None] * 8
    t_ffn = [[0] * NT for _ in range(8)]
    sc_g = [[0] * NT for _ in range(8)]
    sy_g = [[None] * NT for _ in range(8)]

    def load_ffn_slab(fg, waits):
        w_ffn[fg] = gdma(wffn[:, fg % 2, :, :],
                         wb_d[fg].rearrange("p (c f) -> p c f", c=NDC),
                         waits=waits, max_dma_last_dim=8192)

    load_ffn_slab(0, [("V", v_wrep[3]), ("SC", sc_w[15]), ("T", t_zt[1])] +
                  ([dbg_w] if debug else []))
    load_ffn_slab(1, [("T", t_wt[3])])
    gidx = 0
    for fg in range(8):
        for tcp in range(8):
            for half in range(2):
                tc = 2 * tcp + half
                bank = pp[(2 * tcp + half) % 4]
                bw = []
                if tcp >= 2:
                    bw.append(("SC", sc_g[fg][2 * (tcp - 2) + half]))
                elif fg > 0:
                    bw.append(("SC", sc_g[fg - 1][12 + 2 * tcp + half]))
                if tcp == 0 and half == 0:
                    bw += [w_ffn[fg], w_lb, w_bb, ("V", v_wh[3]), c_oneb]
                    if fg == 0:
                        bw += [("T", t_h[3])]
                for dc in range(NDC):
                    P.op("tensor",
                         lambda e, o=bank.ap()[:, 0:256], l=xnF[:, dc, ts(tc, 128)],
                         r=wffn[:, fg % 2, dc, :], s=(dc == 0):
                         e.matmul(o, l, r, start=s, stop=False),
                         waits=(bw if dc == 0 else ()))
                P.op("tensor",
                     lambda e, o=bank.ap()[:, 0:256], l=wh_sb.ap()[:, ts(tc, 128)],
                     r=lb_sb[:, fg, :]:
                     e.matmul(o, l, r, start=False, stop=False))
                t_ffn[fg][tc] = P.op(
                    "tensor",
                    lambda e, o=bank.ap()[:, 0:256], l=oneb_sb.ap(),
                    r=bb_sb.ap()[:, ts(fg, 256)]:
                    e.matmul(o, l, r, start=False, stop=True), inc="T")
                gw = [("T", t_ffn[fg][tc])]
                if gidx >= 2:
                    gw.append(sy_g[(gidx - 2) // 16][(gidx - 2) % 16])
                sc_g[fg][tc] = P.op(
                    "scalar",
                    lambda e, o=gstage.ap()[:, ts(gidx % 2, 256)],
                    i_=bank.ap()[:, 0:256]:
                    e.activation(o, i_, ACT.Silu),
                    waits=gw, inc="SC")
                sy_g[fg][tc] = sdma(gsp_d[tc][:, ts(fg, 256)],
                                    gstage.ap()[:, ts(gidx % 2, 256)],
                                    waits=[("SC", sc_g[fg][tc])])
                gidx += 1
        if fg + 2 < 8:
            load_ffn_slab(fg + 2, [("T", t_ffn[fg][15])])
    t_ffn_end = P.val("T")

    w_wv = [None] * NKV

    def load_v_slab(fc, waits):
        w_wv[fc] = gdma(wsl[:, fc % 2, :, :],
                        wv_d[fc].rearrange("p (c f) -> p c f", c=NDC),
                        waits=waits, max_dma_last_dim=8192)

    load_v_slab(0, [("T", t_ffn[6][15])])
    load_v_slab(1, [("T", t_ffn[7][15])])
    t_v = [[0] * 4 for _ in range(NKV)]
    sc_v = [[0] * 4 for _ in range(NKV)]
    t_vt = [[0] * 2 for _ in range(NKV)]
    v_vt = [[0] * 2 for _ in range(NKV)]
    sy_vsp = [None] * NKV
    v_ones = 0
    for fc in range(NKV):
        for tg in range(4):
            w = []
            if tg == 0:
                w = [w_wv[fc], xn_all]
                if fc == 0:
                    w.append(("SC", sc_g[7][15]))
                else:
                    w.append(("SC", sc_v[fc - 1][3]))
            for dc in range(NDC):
                inc = "T" if dc == NDC - 1 else None
                t_v[fc][tg] = P.op("tensor",
                                   lambda e, o=pp[tg].ap(), l=wsl[:, fc % 2, dc, :],
                                   r=xnF[:, dc, ts(tg, 512)], s=(dc == 0),
                                   t_=(dc == NDC - 1):
                                   e.matmul(o, l, r, start=s, stop=t_),
                                   waits=(w if dc == 0 else ()), inc=inc)
            evw = [("T", t_v[fc][tg])]
            if fc >= 2 and tg == 0:
                evw.append(("T", t_vt[fc - 2][1]))
            sc_v[fc][tg] = P.op("scalar",
                                lambda e, o=stg[:, fc % 2, ts(tg, 512)],
                                i_=pp[tg].ap(), b=bv_sb.ap()[:, fc:fc + 1]:
                                e.activation(o, i_, ACT.Identity, bias=b),
                                waits=evw, inc="SC")
        for g in range(2):
            w = [("SC", sc_v[fc][3])]
            if fc == 0 and g == 0:
                w.append(("V", v_p1_end))
            if fc > 0:
                w.append(("V", v_vt[fc - 1][g]))
            for j in range(8):
                kc = g * 8 + j
                inc = "T" if j == 7 else None
                t_vt[fc][g] = P.op("tensor",
                                   lambda e, o=ptb[g].ap()[:, ts(j, 128)],
                                   i_=stg[:, fc % 2, ts(kc, 128)]:
                                   e.transpose(o, i_, idb_sb.ap()),
                                   waits=(w if j == 0 else ()), inc=inc)
            vw = [("T", t_vt[fc][g])]
            if fc > 0 and g == 0:
                vw.append(sy_vsp[fc - 1])
            v_vt[fc][g] = P.op(
                "vector",
                lambda e, o=vstage[:, g * 8:(g + 1) * 8, 0:128],
                i_=ptb[g].ap().rearrange("p (j t) -> p j t", j=8):
                e.tensor_copy(o, i_),
                waits=vw, inc="V")
        P.op("vector", lambda e, o=vstage[:, :, 128:129]: e.memset(o, 1.0),
             waits=[("V", v_vt[fc][1])])
        v_ones = P.op("vector", lambda e, o=vstage[:, :, 129:132]: e.memset(o, 0.0),
                      inc="V")
        sy_vsp[fc] = sdma(vsp_d[fc], vstage.rearrange("p c d -> p (c d)"),
                          waits=[("V", v_ones)])
        if fc + 2 < NKV:
            load_v_slab(fc + 2, [("T", t_v[fc][3])])

    w_wk = [None] * NKV

    def load_k_slab(fc, waits):
        w_wk[fc] = gdma(wsl[:, fc % 2, :, :],
                        wk_d[fc].rearrange("p (c f) -> p c f", c=NDC),
                        waits=waits, max_dma_last_dim=8192)

    load_k_slab(0, [("T", t_v[2][3])])
    load_k_slab(1, [("T", t_v[3][3])])
    t_k = [[0] * 4 for _ in range(NKV)]
    sc_k = [[0] * 4 for _ in range(NKV)]
    v_kp = [0] * NKV
    for fc in range(NKV):
        for tg in range(4):
            w = []
            if tg == 0:
                w = [w_wk[fc]]
                if fc == 0:
                    w.append(("SC", sc_v[NKV - 1][3]))
                else:
                    w.append(("SC", sc_k[fc - 1][3]))
            for dc in range(NDC):
                inc = "T" if dc == NDC - 1 else None
                t_k[fc][tg] = P.op("tensor",
                                   lambda e, o=pp[tg].ap(), l=wsl[:, fc % 2, dc, :],
                                   r=xnF[:, dc, ts(tg, 512)], s=(dc == 0),
                                   t_=(dc == NDC - 1):
                                   e.matmul(o, l, r, start=s, stop=t_),
                                   waits=(w if dc == 0 else ()), inc=inc)
            evw = [("T", t_k[fc][tg])]
            if fc == 0 and tg == 0:
                evw.append(("V", v_ones))
            if fc >= 2 and tg == 0:
                evw.append(("V", v_kp[fc - 2]))
            sc_k[fc][tg] = P.op("scalar",
                                lambda e, o=stg[:, fc % 2, ts(tg, 512)],
                                i_=pp[tg].ap(), b=bk_sb.ap()[:, fc:fc + 1]:
                                e.activation(o, i_, ACT.Identity, bias=b),
                                waits=evw, inc="SC")
        kw = [("SC", sc_k[fc][3]), c_cos, c_sin]
        swp = [("SC", sc_k[fc][3])]
        if fc == 0:
            kw += [w_la, w_lb, ("T", t_ffn_end), sy_vsp[NKV - 1]]
            swp += [("T", t_ffn_end), sy_vsp[NKV - 1]]
        else:
            swp.append(("V", v_kp[fc - 1]))
        d1 = sdma(ta_sb[0:64, :], stg[64:128, fc % 2, :], waits=swp)
        d2 = sdma(ta_sb[64:128, :], stg[0:64, fc % 2, :])
        P.op("vector", lambda e, o=tb_sb, a=stg[:, fc % 2, :], b=cos_sb.ap():
             e.tensor_tensor(o, a, b, ALU.mult), waits=kw)
        P.op("vector", lambda e, o=ta_sb, a=ta_sb, b=sin_sb.ap():
             e.tensor_tensor(o, a, b, ALU.mult), waits=[d1, d2])
        v_kp[fc] = P.op("vector", lambda e, o=kp_sb[:, fc, :], a=ta_sb, b=tb_sb:
                        e.tensor_add(o, a, b), inc="V")
        if fc + 2 < NKV:
            load_k_slab(fc + 2, [("T", t_k[fc][3])])
    if debug:
        sdma(dbg["kp"], kp_sb.rearrange("p v t -> p (v t)"), waits=[("V", v_kp[3])])

    w_wq = [None] * NH

    def load_q_slab(h, waits):
        w_wq[h] = gdma(wsl[:, h % 2, :, :],
                       wq_d[h].rearrange("p (c f) -> p c f", c=NDC),
                       waits=waits, max_dma_last_dim=8192)

    load_q_slab(0, [("T", t_k[2][3])])
    load_q_slab(1, [("T", t_k[3][3])])
    t_q = [[0] * 4 for _ in range(NH)]
    sc_q = [[0] * 4 for _ in range(NH)]
    v_qp = [0] * NH
    sy_qsp = [None] * NH
    for h in range(NH):
        for tg in range(4):
            w = []
            if tg == 0:
                w = [w_wq[h]]
                if h == 0:
                    w.append(("SC", sc_k[NKV - 1][3]))
                else:
                    w.append(("SC", sc_q[h - 1][3]))
            for dc in range(NDC):
                inc = "T" if dc == NDC - 1 else None
                t_q[h][tg] = P.op("tensor",
                                  lambda e, o=pp[tg].ap(), l=wsl[:, h % 2, dc, :],
                                  r=xnF[:, dc, ts(tg, 512)], s=(dc == 0),
                                  t_=(dc == NDC - 1):
                                  e.matmul(o, l, r, start=s, stop=t_),
                                  waits=(w if dc == 0 else ()), inc=inc)
            evw = [("T", t_q[h][tg])]
            if h == 0 and tg == 0:
                evw.append(("V", v_kp[NKV - 1]))
            if h >= 2 and tg == 0:
                evw.append(sy_qsp[h - 2])
            sc_q[h][tg] = P.op("scalar",
                               lambda e, o=stg[:, h % 2, ts(tg, 512)],
                               i_=pp[tg].ap(), b=bq_sb.ap()[:, h:h + 1]:
                               e.activation(o, i_, ACT.Identity, bias=b),
                               waits=evw, inc="SC")
        qw = [("SC", sc_q[h][3])]
        swp = [("SC", sc_q[h][3])]
        if h == 0:
            swp.append(("V", v_kp[NKV - 1]))
        else:
            swp.append(("V", v_qp[h - 1]))
        d1 = sdma(ta_sb[0:64, :], stg[64:128, h % 2, :], waits=swp)
        d2 = sdma(ta_sb[64:128, :], stg[0:64, h % 2, :])
        P.op("vector", lambda e, o=tb_sb, a=stg[:, h % 2, :], b=cos_sb.ap():
             e.tensor_tensor(o, a, b, ALU.mult), waits=qw)
        P.op("vector", lambda e, o=ta_sb, a=ta_sb, b=sin_sb.ap():
             e.tensor_tensor(o, a, b, ALU.mult), waits=[d1, d2])
        v_qp[h] = P.op("vector", lambda e, o=stg[:, h % 2, :], a=ta_sb, b=tb_sb:
                       e.tensor_add(o, a, b), inc="V")
        sy_qsp[h] = sdma(qsp_d[h], stg[:, h % 2, :], waits=[("V", v_qp[h])])
        if h + 2 < NH:
            load_q_slab(h + 2, [("T", t_q[h][3])])
    t_q_end = P.val("T")
    v_p2_end = P.val("V")
    sc_p2_end = P.val("SC")

    sy_v3 = [None] * NKV
    for fc in range(NKV):
        sy_v3[fc] = sdma(v3_sb[:, fc, :], vsp_d[fc],
                         waits=([("T", t_q_end)] if fc == 0 else ()))

    def prev_head(qg, h):
        hh = qg * NH + h
        if hh == 0:
            return None
        return ((hh - 1) // NH, (hh - 1) % NH)

    t_sc = {}
    sc_e = {}
    t_pv = {}
    v_s = {}
    sy_qh = {}
    g_ld = {}
    x_ld = {}
    v_comb = [0] * NT

    out_s = {}

    def load_gx(tc):
        i4 = tc % 4
        gw = [sy_g[7][tc]]
        xw = []
        if tc < 4:
            gw.append(("T", t_q_end))
            xw.append(("T", t_q_end))
        else:
            gw.append(("V", v_comb[tc - 4]))
            xw.append(out_s[tc - 4])
        g_ld[tc] = sdma(g_sb[:, i4, :], gsp_d[tc], waits=gw)
        x_ld[tc] = sdma(xe_sb[:, i4, :], xr_d[ts(tc, 128), :], waits=xw)

    for tc in range(4):
        load_gx(tc)

    pvb = [pp[2].ap(), pp[3].ap(), pt2.ap(), pex.ap()]
    for qg in range(4):
        for h in range(NH):
            kv = h // 4
            hh = qg * NH + h
            ph = prev_head(qg, h)
            if hh < 2:
                qw = [("T", t_q_end), sy_qsp[NH - 1]]
            else:
                p2 = ((hh - 2) // NH, (hh - 2) % NH)
                qw = [("T", t_sc[(p2[0], p2[1], 15)])]
            sy_qh[(qg, h)] = sdma(qh_sb[:, h % 2, :], qsp_d[h][:, ts(qg, 512)],
                                  waits=qw)
            for kc in range(16):
                w = [sy_qh[(qg, h)]]
                if kc >= 2:
                    w.append(("SC", sc_e[(qg, h, kc - 2)]))
                elif ph is not None:
                    w.append(("SC", sc_e[(ph[0], ph[1], 14 + kc)]))
                elif kc == 0:
                    w.append(("SC", sc_q[NH - 1][3]))
                t_sc[(qg, h, kc)] = P.op(
                    "tensor",
                    lambda e, o=pp[kc % 2].ap(), l=kp_sb[:, kv, ts(kc, 128)],
                    r=qh_sb[:, h % 2, :]:
                    e.matmul(o, l, r, start=True, stop=True),
                    waits=w, inc="T")
                ew = [("T", t_sc[(qg, h, kc)])]
                if kc >= 2:
                    ew.append(("T", t_pv[(qg, h, kc - 2)]))
                elif ph is not None:
                    ew.append(("T", t_pv[(ph[0], ph[1], 14 + kc)]))
                sc_e[(qg, h, kc)] = P.op(
                    "scalar",
                    lambda e, o=pr_sb[:, kc % 2, :], i_=pp[kc % 2].ap():
                    e.activation(o, i_, ACT.Exp, scale=ISCALE),
                    waits=ew, inc="SC")
                pvw = [("SC", sc_e[(qg, h, kc)])]
                if kc == 0:
                    pvw.append(sy_v3[kv])
                    if ph is not None:
                        pvw.append(("V", v_s[(ph[0], ph[1], 3)]))
                    else:
                        pvw.append(("V", v_p2_end))
                for qcj in range(4):
                    inc = "T" if qcj == 3 else None
                    t_pv[(qg, h, kc)] = P.op(
                        "tensor",
                        lambda e, o=pvb[qcj][:, 0:132],
                        l=pr_sb[:, kc % 2, ts(qcj, 128)],
                        r=v3_sb[:, kv, kc * 132:(kc + 1) * 132],
                        s=(kc == 0), t_=(kc == 15):
                        e.matmul(o, l, r, start=s, stop=t_),
                        waits=(pvw if qcj == 0 else ()), inc=inc)
            for qcj in range(4):
                P.op("vector",
                     lambda e, o=rc_sb.ap()[:, qcj:qcj + 1],
                     i_=pvb[qcj][:, 128:129]:
                     e.reciprocal(o, i_),
                     waits=[("T", t_pv[(qg, h, 15)])])
                vsw = []
                if debug and qg == 1 and h == 0:
                    vsw = list(dbg_s)
                v_s[(qg, h, qcj)] = P.op(
                    "vector",
                    lambda e, o=s_sb[:, qcj, ts(h, 128)], i_=pvb[qcj][:, 0:128],
                    r=rc_sb.ap()[:, qcj:qcj + 1]:
                    e.tensor_scalar_mul(o, i_, r),
                    waits=vsw, inc="V")
        if debug and qg == 0:
            dbg_s = []
            for qcj in range(4):
                dbg_s.append(sdma(dbg["s"][ts(qcj, 128), :], s_sb[:, qcj, :],
                                  waits=[("V", v_s[(qg, NH - 1, 3)])]))
        for qcj in range(4):
            tc = qg * 4 + qcj
            i4 = tc % 4
            cw = [("V", v_s[(qg, NH - 1, qcj)]), g_ld[tc], x_ld[tc]]
            if debug and qg == 0:
                cw += dbg_s
            P.op("vector", lambda e, o=s_sb[:, qcj, :], a=s_sb[:, qcj, :],
                 b=g_sb[:, i4, :]: e.tensor_tensor(o, a, b, ALU.mult), waits=cw)
            v_comb[tc] = P.op("vector",
                              lambda e, o=xe_sb[:, i4, :], a=s_sb[:, qcj, :],
                              b=xe_sb[:, i4, :]:
                              e.tensor_tensor(o, a, b, ALU.add), inc="V")
            out_s[tc] = gdma(out_d[ts(tc, 128), :], xe_sb[:, i4, :],
                             waits=[("V", v_comb[tc])])
        if qg + 1 < 4:
            for tc in range(4 * (qg + 1), 4 * (qg + 2)):
                load_gx(tc)

    with nc.Block() as block:
        def make(eng_name):
            def run(eng):
                lastw = {}
                for (fn, waits, inc, amt, dr) in P.q[eng_name]:
                    for (s, v) in waits:
                        if v is None or lastw.get(s, -1) >= v:
                            continue
                        eng.wait_ge(sems[s], v)
                        lastw[s] = v
                    if dr or eng_name == "vector":
                        eng.drain()
                    inst = fn(eng)
                    if inc is not None:
                        inst.then_inc(sems[inc], amt)
                if eng_name == "gpsimd":
                    for tc in range(NT):
                        sname, v = out_s[tc]
                        eng.wait_ge(sems[sname], v)
            return run

        block.sync(make("sync"))
        block.gpsimd(make("gpsimd"))
        block.tensor(make("tensor"))
        block.scalar(make("scalar"))
        block.vector(make("vector"))

    stack.close()
    return nc


_NC_CACHE = {}


def _get_nc(debug=False):
    if debug not in _NC_CACHE:
        _NC_CACHE[debug] = build_program(debug)
    return _NC_CACHE[debug]


def make_in_maps(L0_attn_x, L0_attn_wq, L0_attn_bq, L0_attn_wk, L0_attn_bk,
                 L0_attn_wv, L0_attn_bv, L0_ffn_W_base, L0_ffn_b_base,
                 L0_ffn_W_router, L0_ffn_b_router, L0_ffn_lora_A, L0_ffn_lora_B):
    bf = ml_dtypes.bfloat16
    f32 = np.float32
    x = np.asarray(L0_attn_x, f32)
    wq = np.asarray(L0_attn_wq, f32)
    bq = np.asarray(L0_attn_bq, f32)
    wk = np.asarray(L0_attn_wk, f32)
    bk = np.asarray(L0_attn_bk, f32)
    wv = np.asarray(L0_attn_wv, f32)
    bv = np.asarray(L0_attn_bv, f32)
    wbase = np.asarray(L0_ffn_W_base, f32)
    bbase = np.asarray(L0_ffn_b_base, f32)
    wrout = np.asarray(L0_ffn_W_router, f32)
    brout = np.asarray(L0_ffn_b_router, f32)
    lA = np.asarray(L0_ffn_lora_A, f32)
    lB = np.asarray(L0_ffn_lora_B, f32)

    inv = (10000.0 ** (-np.arange(64, dtype=np.float64) / 64.0))
    ang = np.arange(S, dtype=np.float64)[:, None] * inv[None, :]
    cost = np.cos(ang).T.astype(f32)
    sint = np.sin(ang).T.astype(f32)
    cos2 = np.concatenate([cost, cost], 0).astype(bf)
    sin2 = np.concatenate([-sint, sint], 0).astype(bf)
    idb = np.eye(128, dtype=bf)
    idf = np.eye(128, dtype=f32)
    oneb = np.ones((1, 128), dtype=bf)
    emat = np.kron(np.eye(8, dtype=f32), np.ones((1, 16), f32)).astype(bf)

    wbT = np.ascontiguousarray(wbase.T)
    A2 = lA.transpose(1, 0, 2).reshape(DM, 128)
    B2 = lB.reshape(128, DM)
    wrT = np.ascontiguousarray(wrout.T)

    def slab(wsl_, nh):
        return np.ascontiguousarray(
            wsl_.reshape(NDC, 128, nh, 128).transpose(2, 1, 0, 3).reshape(nh, 128, DM))

    in_maps = []
    for c in range(8):
        b, fh = c // 2, c % 2
        fsl = slice(fh * FPC, (fh + 1) * FPC)
        ksl = slice(fh * 512, (fh + 1) * 512)
        im = {
            "x": np.ascontiguousarray(x[b]),
            "xr": np.ascontiguousarray(x[b][:, fsl]),
            "wq": slab(wq[:, fsl], NH),
            "wk": slab(wk[:, ksl], NKV),
            "wv": slab(wv[:, ksl], NKV),
            "wb": np.ascontiguousarray(
                wbT[:, fsl].reshape(NDC, 128, 8, 256).transpose(2, 1, 0, 3)
                .reshape(8, 128, 8192)),
            "wr": np.ascontiguousarray(
                wrT.reshape(NDC, 128, 8).transpose(1, 0, 2).reshape(128, 256)),
            "la": np.ascontiguousarray(
                A2.reshape(NDC, 128, 128).transpose(1, 0, 2).reshape(128, DM)),
            "lb": np.ascontiguousarray(B2[:, fsl]),
            "bq": np.ascontiguousarray(bq[fsl].reshape(16, 128).T),
            "bk": np.ascontiguousarray(bk[ksl].reshape(4, 128).T),
            "bv": np.ascontiguousarray(bv[ksl].reshape(4, 128).T),
            "br": np.ascontiguousarray(brout[:, None]),
            "bb": np.ascontiguousarray(bbase[fsl][None, :]),
            "cos2": cos2, "sin2": sin2, "idb": idb, "idf": idf,
            "oneb": oneb, "emat": emat,
        }
        in_maps.append(im)
    return in_maps


def kernel(L0_attn_x, L0_attn_wq, L0_attn_bq, L0_attn_wk, L0_attn_bk,
           L0_attn_wv, L0_attn_bv, L0_ffn_W_base, L0_ffn_b_base,
           L0_ffn_W_router, L0_ffn_b_router, L0_ffn_lora_A, L0_ffn_lora_B):
    in_maps = make_in_maps(L0_attn_x, L0_attn_wq, L0_attn_bq, L0_attn_wk, L0_attn_bk,
                           L0_attn_wv, L0_attn_bv, L0_ffn_W_base, L0_ffn_b_base,
                           L0_ffn_W_router, L0_ffn_b_router, L0_ffn_lora_A,
                           L0_ffn_lora_B)
    nc = _get_nc(False)
    res = run_bass_kernel_spmd(nc, in_maps, core_ids=list(range(8)))
    out = np.empty((4, S, DM), dtype=np.float32)
    for c in range(8):
        b, fh = c // 2, c % 2
        out[b, :, fh * FPC:(fh + 1) * FPC] = res.results[c]["out"]
    return out
